# version 01 (final)
# baseline (speedup 1.0000x reference)
import contextlib
import numpy as np
import concourse.bass as bass
import concourse.mybir as mybir
from concourse.bass_utils import run_bass_kernel_spmd

F32 = mybir.dt.float32
AF = mybir.ActivationFunctionType
AX = mybir.AxisListType
NCORE = 8
TOK = 8192
TPC = 1024
D = 2048
CONV_DIM = 4096
KCONV = 4
HV = 16
DK = 128
SEQ = 4096
CH = 64
NCH = SEQ // CH
QSCALE = float(DK) ** -0.5
L2EPS = 1e-6
RMSEPS = 1e-6
LNEPS = 1e-5


class Prog:

    def __init__(self):
        self.nc = bass.Bass("TRN2", target_bir_lowering=False)
        self.ops = []
        self._stack = contextlib.ExitStack()

    def dram(self, name, shape, kind):
        return self.nc.dram_tensor(name, shape, F32, kind=kind).ap()

    def sb(self, name, shape):
        return self._stack.enter_context(self.nc.sbuf_tensor(name, shape, F32))

    def ps(self, name, shape):
        return self._stack.enter_context(self.nc.psum_tensor(name, shape, F32))

    def E(self, eng, fn, inc=1):
        self.ops.append((eng, inc, fn))

    def dma(self, out, in_):
        self.E("sync", lambda o=out, i=in_: self.nc.sync.dma_start(out=o, in_=i), 16)

    def finish(self):
        nc = self.nc
        tick = self._stack.enter_context(nc.semaphore("tick"))
        pre = []
        c = 0
        for (_, inc, _) in self.ops:
            pre.append(c)
            c += inc
        total = c
        blk = self._stack.enter_context(nc.Block())
        engs = {"sync": nc.sync, "vector": nc.vector, "scalar": nc.scalar,
                "tensor": nc.tensor, "gpsimd": nc.gpsimd}

        def mksec(name):
            def sec(_):
                eng = engs[name]
                for idx, (e, inc, fn) in enumerate(self.ops):
                    if e != name:
                        continue
                    if pre[idx] > 0:
                        eng.wait_ge(tick, pre[idx])
                    fn().then_inc(tick, inc)
                if name == "sync":
                    eng.wait_ge(tick, total)
            return sec

        used = set(e for (e, _, _) in self.ops)
        used.add("sync")
        for name in used:
            getattr(blk, name)(mksec(name))
        self._stack.close()
        return nc


_cache = {}


def run(key, builder, in_maps):
    if key not in _cache:
        _cache[key] = builder()
    res = run_bass_kernel_spmd(_cache[key], in_maps, core_ids=list(range(NCORE)))
    return res.results


def build_proj():
    p = Prog()
    nc = p.nc
    xT = p.dram("xT", [D, TPC], "ExternalInput")
    WqkvT = p.dram("WqkvT", [D, CONV_DIM], "ExternalInput")
    WzT = p.dram("WzT", [D, D], "ExternalInput")
    WabT = p.dram("WabT", [D, 32], "ExternalInput")
    bqkv = p.dram("bqkv", [1, CONV_DIM], "ExternalInput")
    bz = p.dram("bz", [1, D], "ExternalInput")
    bab = p.dram("bab", [1, 32], "ExternalInput")
    mixed = p.dram("mixed", [TPC, CONV_DIM], "ExternalOutput")
    z = p.dram("z", [TPC, D], "ExternalOutput")
    ab = p.dram("ab", [TPC, 32], "ExternalOutput")

    xsb = p.sb("xsb", [128, 16 * TPC])
    wsb = p.sb("wsb", [128, 16 * 512])
    bsb = p.sb("bsb", [1, CONV_DIM])
    ones = p.sb("ones", [1, 128])
    ot = p.sb("ot", [128, 512])
    pm = p.ps("pm", [128, 512])

    p.E("vector", lambda: nc.vector.memset(ones[:, :], 1.0))
    for k in range(16):
        p.dma(xsb[:, k * TPC:(k + 1) * TPC], xT[k * 128:(k + 1) * 128, :])

    def gemm(WT, bias_d, out_d, N):
        nch = N // 512 if N >= 512 else 1
        w = min(N, 512)
        p.dma(bsb[0:1, 0:N], bias_d[0:1, 0:N])
        for n in range(nch):
            for k in range(16):
                p.dma(wsb[:, k * 512:k * 512 + w],
                      WT[k * 128:(k + 1) * 128, n * w:(n + 1) * w])
            for m in range(8):
                for k in range(16):
                    p.E("tensor", lambda m=m, k=k, w=w:
                        nc.tensor.matmul(pm[:, 0:w],
                                         xsb[:, k * TPC + m * 128:k * TPC + (m + 1) * 128],
                                         wsb[:, k * 512:k * 512 + w],
                                         start=(k == 0), stop=False))
                p.E("tensor", lambda m=m, n=n, w=w:
                    nc.tensor.matmul(pm[:, 0:w], ones[0:1, 0:128],
                                     bsb[0:1, n * w:(n + 1) * w],
                                     start=False, stop=True))
                p.E("vector", lambda w=w: nc.vector.tensor_copy(ot[:, 0:w], pm[:, 0:w]))
                p.dma(out_d[m * 128:(m + 1) * 128, n * w:(n + 1) * w], ot[:, 0:w])

    gemm(WqkvT, bqkv, mixed, CONV_DIM)
    gemm(WzT, bz, z, D)
    gemm(WabT, bab, ab, 32)
    return p.finish()


def build_conv():
    p = Prog()
    nc = p.nc
    xc = p.dram("xc", [512, TOK], "ExternalInput")
    cw = p.dram("cw", [512, KCONV], "ExternalInput")
    cb = p.dram("cb", [512, 1], "ExternalInput")
    yout = p.dram("yout", [512, TOK], "ExternalOutput")
    xs = p.sb("xs", [128, TOK])
    acc = p.sb("acc", [128, TOK])
    tmp = p.sb("tmp", [128, SEQ])
    w = p.sb("w", [128, KCONV])
    b = p.sb("b", [128, 1])
    for t in range(4):
        r = slice(t * 128, (t + 1) * 128)
        p.dma(xs[:, :], xc[r, :])
        p.dma(w[:, :], cw[r, :])
        p.dma(b[:, :], cb[r, :])
        for s in range(2):
            base = s * SEQ
            p.E("vector", lambda base=base:
                nc.vector.tensor_scalar_mul(acc[:, base:base + SEQ],
                                            xs[:, base:base + SEQ], w[:, 3:4]))
            for j in (2, 1, 0):
                sh = 3 - j
                p.E("vector", lambda base=base, j=j, sh=sh:
                    nc.vector.tensor_scalar_mul(tmp[:, 0:SEQ - sh],
                                                xs[:, base:base + SEQ - sh], w[:, j:j + 1]))
                p.E("vector", lambda base=base, sh=sh:
                    nc.vector.tensor_add(acc[:, base + sh:base + SEQ],
                                         acc[:, base + sh:base + SEQ], tmp[:, 0:SEQ - sh]))
        p.E("vector", lambda: nc.vector.tensor_scalar_add(acc[:, :], acc[:, :], b[:, 0:1]))
        p.E("scalar", lambda: nc.scalar.activation(acc[:, :], acc[:, :], AF.Silu))
        p.dma(yout[r, :], acc[:, :])
    return p.finish()


def build_gates():
    p = Prog()
    nc = p.nc
    aT = p.dram("aT", [16, TOK], "ExternalInput")
    bT = p.dram("bT", [16, TOK], "ExternalInput")
    dtb = p.dram("dtb", [16, 1], "ExternalInput")
    Al = p.dram("Al", [16, 1], "ExternalInput")
    beta = p.dram("beta", [16, TOK], "ExternalOutput")
    eg = p.dram("eg", [16, TOK], "ExternalOutput")
    a = p.sb("a", [16, TOK])
    bt = p.sb("bt", [16, TOK])
    r = p.sb("r", [16, TOK])
    t2 = p.sb("t2", [16, TOK])
    db = p.sb("db", [16, 1])
    al = p.sb("al", [16, 1])
    p.dma(a[:, :], aT[:, :])
    p.dma(bt[:, :], bT[:, :])
    p.dma(db[:, :], dtb[:, :])
    p.dma(al[:, :], Al[:, :])
    p.E("scalar", lambda: nc.scalar.activation(bt[:, :], bt[:, :], AF.Sigmoid))
    p.dma(beta[:, :], bt[:, :])
    p.E("vector", lambda: nc.vector.tensor_scalar_add(a[:, :], a[:, :], db[:, 0:1]))
    p.E("vector", lambda: nc.vector.tensor_relu(r[:, :], a[:, :]))
    p.E("scalar", lambda: nc.scalar.activation(t2[:, :], a[:, :], AF.Abs))
    p.E("scalar", lambda: nc.scalar.activation(t2[:, :], t2[:, :], AF.Exp, scale=-1.0))
    p.E("scalar", lambda: nc.scalar.activation(t2[:, :], t2[:, :], AF.Ln, bias=1.0))
    p.E("vector", lambda: nc.vector.tensor_add(r[:, :], r[:, :], t2[:, :]))
    p.E("scalar", lambda: nc.scalar.activation(al[:, :], al[:, :], AF.Exp))
    p.E("scalar", lambda: nc.scalar.mul(al[:, :], al[:, :], -1.0))
    p.E("vector", lambda: nc.vector.tensor_scalar_mul(r[:, :], r[:, :], al[:, 0:1]))
    p.E("scalar", lambda: nc.scalar.activation(r[:, :], r[:, :], AF.Exp))
    p.dma(eg[:, :], r[:, :])
    return p.finish()


def build_l2n():
    p = Prog()
    nc = p.nc
    hin = p.dram("hin", [4 * 128, TOK], "ExternalInput")
    sc = p.dram("sc", [1, 4], "ExternalInput")
    hout = p.dram("hout", [4 * 128, TOK], "ExternalOutput")
    h = p.sb("h", [128, TOK])
    sq = p.sb("sq", [128, TOK])
    srow = p.sb("srow", [1, TOK])
    scs = p.sb("scs", [1, 4])
    onesc = p.sb("onesc", [128, 1])
    onesr = p.sb("onesr", [1, 128])
    psS = p.ps("psS", [1, 512])
    psB = p.ps("psB", [128, 512])
    p.E("vector", lambda: nc.vector.memset(onesc[:, :], 1.0))
    p.E("vector", lambda: nc.vector.memset(onesr[:, :], 1.0))
    p.dma(scs[:, :], sc[:, :])
    for i in range(4):
        r = slice(i * 128, (i + 1) * 128)
        p.dma(h[:, :], hin[r, :])
        p.E("scalar", lambda: nc.scalar.square(sq[:, :], h[:, :]))
        for j in range(16):
            p.E("tensor", lambda j=j:
                nc.tensor.matmul(psS[:, :], onesc[:, 0:1], sq[:, j * 512:(j + 1) * 512],
                                 start=True, stop=True))
            p.E("vector", lambda j=j:
                nc.vector.tensor_copy(srow[0:1, j * 512:(j + 1) * 512], psS[0:1, :]))
        p.E("vector", lambda: nc.vector.tensor_scalar_add(srow[:, :], srow[:, :], L2EPS))
        p.E("scalar", lambda: nc.scalar.activation(srow[:, :], srow[:, :], AF.Sqrt))
        p.E("vector", lambda: nc.vector.reciprocal(srow[:, :], srow[:, :]))
        p.E("vector", lambda i=i:
            nc.vector.tensor_scalar_mul(srow[:, :], srow[:, :], scs[0:1, i:i + 1]))
        for j in range(16):
            p.E("tensor", lambda j=j:
                nc.tensor.matmul(psB[:, :], onesr[0:1, :], srow[0:1, j * 512:(j + 1) * 512],
                                 start=True, stop=True))
            p.E("vector", lambda j=j:
                nc.vector.tensor_mul(h[:, j * 512:(j + 1) * 512],
                                     h[:, j * 512:(j + 1) * 512], psB[:, :]))
        p.dma(hout[r, :], h[:, :])
    return p.finish()


def build_scan():
    p = Prog()
    nc = p.nc
    Q = p.dram("Q", [4 * 128, CH], "ExternalInput")
    Kc = p.dram("Kc", [4 * 128, CH], "ExternalInput")
    Kr = p.dram("Kr", [4, CH * 128], "ExternalInput")
    Vc = p.dram("Vc", [4 * 128, CH], "ExternalInput")
    EG = p.dram("EG", [4, CH], "ExternalInput")
    BB = p.dram("BB", [4, CH], "ExternalInput")
    Ident = p.dram("Ident", [128, 128], "ExternalInput")
    Sin = p.dram("Sin", [4 * 128, 128], "ExternalInput")
    Sout = p.dram("Sout", [4 * 128, 128], "ExternalOutput")
    O = p.dram("O", [4 * 128, CH], "ExternalOutput")
    onesr = p.sb("onesr", [1, 128])
    ident = p.sb("ident", [128, 128])
    p.E("vector", lambda: nc.vector.memset(onesr[:, :], 1.0))
    p.dma(ident[:, :], Ident[:, :])
    qc = [p.sb(f"qc{i}", [128, CH]) for i in range(4)]
    kc = [p.sb(f"kc{i}", [128, CH]) for i in range(4)]
    kr = [p.sb(f"kr{i}", [1, CH * 128]) for i in range(4)]
    vc = [p.sb(f"vc{i}", [128, CH]) for i in range(4)]
    egr = [p.sb(f"eg{i}", [1, CH]) for i in range(4)]
    br = [p.sb(f"br{i}", [1, CH]) for i in range(4)]
    S = [p.sb(f"S{i}", [128, 128]) for i in range(4)]
    egb = [p.sb(f"egb{i}", [128, CH]) for i in range(4)]
    bb = [p.sb(f"bb{i}", [128, CH]) for i in range(4)]
    oc = [p.sb(f"oc{i}", [128, CH]) for i in range(4)]
    dcol = [p.sb(f"dc{i}", [128, 1]) for i in range(4)]
    drow = [p.sb(f"dr{i}", [1, 128]) for i in range(4)]
    psX = [p.ps(f"psX{i}", [128, 128]) for i in range(4)]
    psB = [p.ps(f"psB{i}", [128, 128]) for i in range(4)]
    for i in range(4):
        r = slice(i * 128, (i + 1) * 128)
        p.dma(qc[i][:, :], Q[r, :])
        p.dma(kc[i][:, :], Kc[r, :])
        p.dma(kr[i][:, :], Kr[i:i + 1, :])
        p.dma(vc[i][:, :], Vc[r, :])
        p.dma(egr[i][:, :], EG[i:i + 1, :])
        p.dma(br[i][:, :], BB[i:i + 1, :])
        p.dma(S[i][:, :], Sin[r, :])
        p.E("tensor", lambda i=i:
            nc.tensor.matmul(psB[i][:, 0:CH], onesr[0:1, :], egr[i][0:1, :],
                             start=True, stop=True))
        p.E("vector", lambda i=i: nc.vector.tensor_copy(egb[i][:, :], psB[i][:, 0:CH]))
        p.E("tensor", lambda i=i:
            nc.tensor.matmul(psB[i][:, 0:CH], onesr[0:1, :], br[i][0:1, :],
                             start=True, stop=True))
        p.E("vector", lambda i=i: nc.vector.tensor_copy(bb[i][:, :], psB[i][:, 0:CH]))
    for t in range(CH):
        for i in range(4):
            cs = slice(t * 128, (t + 1) * 128)
            p.E("vector", lambda i=i, t=t:
                nc.vector.tensor_scalar_mul(S[i][:, :], S[i][:, :], egb[i][:, t:t + 1]))
            p.E("tensor", lambda i=i, t=t:
                nc.tensor.matmul(psX[i][:, 0:1], S[i][:, :], kc[i][:, t:t + 1],
                                 start=True, stop=True))
            p.E("vector", lambda i=i, t=t:
                nc.vector.tensor_sub(dcol[i][:, :], vc[i][:, t:t + 1], psX[i][:, 0:1]))
            p.E("vector", lambda i=i, t=t:
                nc.vector.tensor_scalar_mul(dcol[i][:, :], dcol[i][:, :], bb[i][:, t:t + 1]))
            p.E("tensor", lambda i=i:
                nc.tensor.matmul(psX[i][0:1, :], dcol[i][:, :], ident[:, :],
                                 start=True, stop=True))
            p.E("vector", lambda i=i:
                nc.vector.tensor_copy(drow[i][:, :], psX[i][0:1, :]))
            p.E("tensor", lambda i=i, cs=cs:
                nc.tensor.matmul(psB[i][:, :], kr[i][0:1, cs], drow[i][0:1, :],
                                 start=True, stop=True))
            p.E("vector", lambda i=i:
                nc.vector.tensor_add(S[i][:, :], S[i][:, :], psB[i][:, :]))
            p.E("tensor", lambda i=i, t=t:
                nc.tensor.matmul(psX[i][:, 0:1], S[i][:, :], qc[i][:, t:t + 1],
                                 start=True, stop=True))
            p.E("vector", lambda i=i, t=t:
                nc.vector.tensor_copy(oc[i][:, t:t + 1], psX[i][:, 0:1]))
    for i in range(4):
        r = slice(i * 128, (i + 1) * 128)
        p.dma(Sout[r, :], S[i][:, :])
        p.dma(O[r, :], oc[i][:, :])
    return p.finish()


def build_rmsgate():
    p = Prog()
    nc = p.nc
    core = p.dram("core", [TPC, D], "ExternalInput")
    zin = p.dram("zin", [TPC, D], "ExternalInput")
    nw = p.dram("nw", [1, D], "ExternalInput")
    out = p.dram("out", [TPC, D], "ExternalOutput")
    c = p.sb("c", [128, D])
    zs = p.sb("zs", [128, D])
    tmp = p.sb("tmp", [128, 128])
    v = p.sb("v", [128, 1])
    nwr = p.sb("nwr", [1, D])
    nwb = p.sb("nwb", [128, D])
    onesr = p.sb("onesr", [1, 128])
    psB = p.ps("psB", [128, 512])
    p.E("vector", lambda: nc.vector.memset(onesr[:, :], 1.0))
    p.dma(nwr[:, :], nw[:, :])
    for j in range(4):
        p.E("tensor", lambda j=j:
            nc.tensor.matmul(psB[:, :], onesr[0:1, :], nwr[0:1, j * 512:(j + 1) * 512],
                             start=True, stop=True))
        p.E("vector", lambda j=j:
            nc.vector.tensor_copy(nwb[:, j * 512:(j + 1) * 512], psB[:, :]))
    for m in range(8):
        r = slice(m * 128, (m + 1) * 128)
        p.dma(c[:, :], core[r, :])
        p.dma(zs[:, :], zin[r, :])
        p.E("scalar", lambda: nc.scalar.activation(zs[:, :], zs[:, :], AF.Silu))
        for h in range(16):
            hs = slice(h * 128, (h + 1) * 128)
            p.E("scalar", lambda hs=hs: nc.scalar.square(tmp[:, :], c[:, hs]))
            p.E("vector", lambda: nc.vector.reduce_sum(v[:, 0:1], tmp[:, :], axis=AX.X))
            p.E("scalar", lambda: nc.scalar.mul(v[:, :], v[:, :], 1.0 / 128.0))
            p.E("vector", lambda: nc.vector.tensor_scalar_add(v[:, :], v[:, :], RMSEPS))
            p.E("scalar", lambda: nc.scalar.activation(v[:, :], v[:, :], AF.Sqrt))
            p.E("vector", lambda: nc.vector.reciprocal(v[:, :], v[:, :]))
            p.E("vector", lambda hs=hs:
                nc.vector.tensor_scalar_mul(c[:, hs], c[:, hs], v[:, 0:1]))
        p.E("vector", lambda: nc.vector.tensor_mul(c[:, :], c[:, :], nwb[:, :]))
        p.E("vector", lambda: nc.vector.tensor_mul(c[:, :], c[:, :], zs[:, :]))
        p.dma(out[r, :], c[:, :])
    return p.finish()


def build_linproj(KD, name):
    p = Prog()
    nc = p.nc
    nk = KD // 128
    hT = p.dram("hT", [KD, TPC], "ExternalInput")
    WT = p.dram("WT", [KD, D], "ExternalInput")
    bias = p.dram("bias", [1, D], "ExternalInput")
    resid = p.dram("resid", [TPC, D], "ExternalInput")
    out = p.dram("out", [TPC, D], "ExternalOutput")
    hsb = p.sb("hsb", [128, nk * 128])
    wsb = p.sb("wsb", [128, nk * 512])
    bsb = p.sb("bsb", [1, D])
    ones = p.sb("ones", [1, 128])
    osb = p.sb("osb", [128, D])
    rsb = p.sb("rsb", [128, D])
    tmp = p.sb("tmp", [128, D])
    mv = p.sb("mv", [128, 1])
    pm = p.ps("pm", [128, 512])
    p.E("vector", lambda: nc.vector.memset(ones[:, :], 1.0))
    p.dma(bsb[:, :], bias[:, :])
    for n in range(4):
        for k in range(nk):
            p.dma(wsb[:, k * 512:(k + 1) * 512],
                  WT[k * 128:(k + 1) * 128, n * 512:(n + 1) * 512])
        for m in range(8):
            r = slice(m * 128, (m + 1) * 128)
            ns = slice(n * 512, (n + 1) * 512)
            for k in range(nk):
                p.dma(hsb[:, k * 128:(k + 1) * 128],
                      hT[k * 128:(k + 1) * 128, m * 128:(m + 1) * 128])
            for k in range(nk):
                p.E("tensor", lambda m=m, k=k:
                    nc.tensor.matmul(pm[:, :],
                                     hsb[:, k * 128:(k + 1) * 128],
                                     wsb[:, k * 512:(k + 1) * 512],
                                     start=(k == 0), stop=False))
            p.E("tensor", lambda n=n:
                nc.tensor.matmul(pm[:, :], ones[0:1, :], bsb[0:1, n * 512:(n + 1) * 512],
                                 start=False, stop=True))
            p.E("vector", lambda ns=ns, m=m:
                nc.vector.tensor_copy(tmp[:, 0:512], pm[:, :]))
            p.dma(out[r, ns], tmp[:, 0:512])
    for m in range(8):
        r = slice(m * 128, (m + 1) * 128)
        p.dma(osb[:, :], out[r, :])
        p.dma(rsb[:, :], resid[r, :])
        p.E("vector", lambda: nc.vector.tensor_add(osb[:, :], osb[:, :], rsb[:, :]))
        p.E("vector", lambda: nc.vector.reduce_sum(mv[:, 0:1], osb[:, :], axis=AX.X))
        p.E("scalar", lambda: nc.scalar.mul(mv[:, :], mv[:, :], 1.0 / D))
        p.E("vector", lambda: nc.vector.tensor_scalar_sub(osb[:, :], osb[:, :], mv[:, 0:1]))
        p.E("scalar", lambda: nc.scalar.square(tmp[:, :], osb[:, :]))
        p.E("vector", lambda: nc.vector.reduce_sum(mv[:, 0:1], tmp[:, :], axis=AX.X))
        p.E("scalar", lambda: nc.scalar.mul(mv[:, :], mv[:, :], 1.0 / D))
        p.E("vector", lambda: nc.vector.tensor_scalar_add(mv[:, :], mv[:, :], LNEPS))
        p.E("scalar", lambda: nc.scalar.activation(mv[:, :], mv[:, :], AF.Sqrt))
        p.E("vector", lambda: nc.vector.reciprocal(mv[:, :], mv[:, :]))
        p.E("vector", lambda: nc.vector.tensor_scalar_mul(osb[:, :], osb[:, :], mv[:, 0:1]))
        p.dma(out[r, :], osb[:, :])
    return p.finish()


def build_ffn1():
    p = Prog()
    nc = p.nc
    xT = p.dram("xT", [D, TPC], "ExternalInput")
    W1T = p.dram("W1T", [D, 8192], "ExternalInput")
    W3T = p.dram("W3T", [D, 8192], "ExternalInput")
    hout = p.dram("hout", [TPC, 8192], "ExternalOutput")
    xsb = p.sb("xsb", [128, 16 * TPC])
    w1 = p.sb("w1", [128, 16 * 512])
    w3 = p.sb("w3", [128, 16 * 512])
    g = p.sb("g", [128, 512])
    o = p.sb("o", [128, 512])
    p1 = p.ps("p1", [128, 512])
    p3 = p.ps("p3", [128, 512])
    for k in range(16):
        p.dma(xsb[:, k * TPC:(k + 1) * TPC], xT[k * 128:(k + 1) * 128, :])
    for n in range(16):
        ns = slice(n * 512, (n + 1) * 512)
        for k in range(16):
            p.dma(w1[:, k * 512:(k + 1) * 512], W1T[k * 128:(k + 1) * 128, ns])
            p.dma(w3[:, k * 512:(k + 1) * 512], W3T[k * 128:(k + 1) * 128, ns])
        for m in range(8):
            for k in range(16):
                p.E("tensor", lambda m=m, k=k:
                    nc.tensor.matmul(p1[:, :],
                                     xsb[:, k * TPC + m * 128:k * TPC + (m + 1) * 128],
                                     w1[:, k * 512:(k + 1) * 512],
                                     start=(k == 0), stop=(k == 15)))
            for k in range(16):
                p.E("tensor", lambda m=m, k=k:
                    nc.tensor.matmul(p3[:, :],
                                     xsb[:, k * TPC + m * 128:k * TPC + (m + 1) * 128],
                                     w3[:, k * 512:(k + 1) * 512],
                                     start=(k == 0), stop=(k == 15)))
            p.E("scalar", lambda: nc.scalar.activation(g[:, :], p1[:, :], AF.Silu))
            p.E("vector", lambda: nc.vector.tensor_mul(o[:, :], g[:, :], p3[:, :]))
            p.dma(hout[m * 128:(m + 1) * 128, ns], o[:, :])
    return p.finish()


def build_fwht():
    p = Prog()
    nc = p.nc
    hin = p.dram("hin", [TPC, 8192], "ExternalInput")
    hout = p.dram("hout", [TPC, 8192], "ExternalOutput")
    A = p.sb("A", [128, 8192])
    B = p.sb("B", [128, 8192])
    for m in range(8):
        r = slice(m * 128, (m + 1) * 128)
        p.dma(A[:, :], hin[r, :])
        src, dst = A, B
        for st in range(13):
            h = 1 << st
            nblk = 8192 // (2 * h)
            sap = src[:, :].rearrange("p (n two h) -> p n two h", two=2, h=h)
            dap = dst[:, :].rearrange("p (n two h) -> p n two h", two=2, h=h)
            p.E("vector", lambda sap=sap, dap=dap:
                nc.vector.tensor_add(dap[:, :, 0, :], sap[:, :, 0, :], sap[:, :, 1, :]))
            p.E("vector", lambda sap=sap, dap=dap:
                nc.vector.tensor_sub(dap[:, :, 1, :], sap[:, :, 0, :], sap[:, :, 1, :]))
            src, dst = dst, src
        p.E("scalar", lambda src=src: nc.scalar.mul(src[:, :], src[:, :], float(8192) ** -0.5))
        p.dma(hout[r, :], src[:, :])
    return p.finish()


def _c(a):
    return np.ascontiguousarray(np.asarray(a, dtype=np.float32))


def kernel(L0_attn_hidden_states, L0_attn_Wqkv, L0_attn_bqkv, L0_attn_Wz,
           L0_attn_bz, L0_attn_Wb, L0_attn_bb, L0_attn_Wa, L0_attn_ba,
           L0_attn_conv_w, L0_attn_conv_b, L0_attn_dt_bias, L0_attn_A_log,
           L0_attn_norm_w, L0_attn_Wout, L0_attn_bout,
           L0_ffn_W1, L0_ffn_W2, L0_ffn_W3):
    x = _c(L0_attn_hidden_states)
    xf = x.reshape(TOK, D)
    sl = [slice(c * TPC, (c + 1) * TPC) for c in range(NCORE)]

    WqkvT = _c(np.asarray(L0_attn_Wqkv).T)
    WzT = _c(np.asarray(L0_attn_Wz).T)
    WabT = _c(np.concatenate([np.asarray(L0_attn_Wb), np.asarray(L0_attn_Wa)], 0).T)
    bqkv = _c(L0_attn_bqkv).reshape(1, -1)
    bz = _c(L0_attn_bz).reshape(1, -1)
    bab = _c(np.concatenate([np.asarray(L0_attn_bb), np.asarray(L0_attn_ba)])).reshape(1, -1)

    res = run("proj", build_proj,
              [{"xT": _c(xf[s].T), "WqkvT": WqkvT, "WzT": WzT, "WabT": WabT,
                "bqkv": bqkv, "bz": bz, "bab": bab} for s in sl])
    mixed = np.concatenate([r["mixed"] for r in res], 0)
    zfull = np.concatenate([r["z"] for r in res], 0)
    abf = np.concatenate([r["ab"] for r in res], 0)

    mixT = _c(mixed.T)
    cw = _c(L0_attn_conv_w)
    cb = _c(L0_attn_conv_b).reshape(-1, 1)
    res = run("conv", build_conv,
              [{"xc": _c(mixT[c * 512:(c + 1) * 512]),
                "cw": _c(cw[c * 512:(c + 1) * 512]),
                "cb": _c(cb[c * 512:(c + 1) * 512])} for c in range(NCORE)])
    convT = np.concatenate([r["yout"] for r in res], 0)

    gin = {"bT": _c(abf[:, 0:16].T), "aT": _c(abf[:, 16:32].T),
           "dtb": _c(L0_attn_dt_bias).reshape(16, 1),
           "Al": _c(L0_attn_A_log).reshape(16, 1)}
    res = run("gates", build_gates, [gin] * NCORE)
    beta16 = res[0]["beta"]
    eg16 = res[0]["eg"]

    inst_rows = []
    scs = []
    for idx in range(32):
        qk = idx // 16
        b8 = idx % 16
        base = (1024 if qk else 0) + (b8 % 8) * 128
        inst_rows.append(convT[base:base + 128])
        scs.append(1.0 if qk else QSCALE)
    res = run("l2n", build_l2n,
              [{"hin": _c(np.concatenate(inst_rows[4 * c:4 * c + 4], 0)),
                "sc": _c(np.array(scs[4 * c:4 * c + 4])).reshape(1, 4)}
               for c in range(NCORE)])
    qkn = np.concatenate([r["hout"] for r in res], 0)

    S = np.zeros((32, 128, 128), np.float32)
    core_attn = np.zeros((TOK, D), np.float32)
    ident = np.eye(128, dtype=np.float32)
    for ck in range(NCH):
        t0 = ck * CH
        maps = []
        for c in range(NCORE):
            Qs, Ks, Krs, Vcs, EGs, Bs, Sins = [], [], [], [], [], [], []
            for i in range(4):
                pp = 4 * c + i
                b, h = pp // 16, pp % 16
                col = slice(b * SEQ + t0, b * SEQ + t0 + CH)
                qi = (b * 8 + h // 2) * 128
                ki = (16 + b * 8 + h // 2) * 128
                Qs.append(qkn[qi:qi + 128, col])
                Ks.append(qkn[ki:ki + 128, col])
                Krs.append(qkn[ki:ki + 128, col].T.reshape(1, -1))
                Vcs.append(convT[2048 + h * 128:2048 + (h + 1) * 128, col])
                EGs.append(eg16[h, col].reshape(1, -1))
                Bs.append(beta16[h, col].reshape(1, -1))
                Sins.append(S[pp])
            maps.append({"Q": _c(np.concatenate(Qs, 0)), "Kc": _c(np.concatenate(Ks, 0)),
                         "Kr": _c(np.concatenate(Krs, 0)), "Vc": _c(np.concatenate(Vcs, 0)),
                         "EG": _c(np.concatenate(EGs, 0)), "BB": _c(np.concatenate(Bs, 0)),
                         "Ident": ident, "Sin": _c(np.concatenate(Sins, 0))})
        res = run("scan", build_scan, maps)
        for c in range(NCORE):
            for i in range(4):
                pp = 4 * c + i
                b, h = pp // 16, pp % 16
                S[pp] = res[c]["Sout"][i * 128:(i + 1) * 128]
                core_attn[b * SEQ + t0:b * SEQ + t0 + CH, h * 128:(h + 1) * 128] = \
                    res[c]["O"][i * 128:(i + 1) * 128].T

    nwrow = _c(np.tile(np.asarray(L0_attn_norm_w), HV)).reshape(1, D)
    res = run("rmsgate", build_rmsgate,
              [{"core": _c(core_attn[s]), "zin": _c(zfull[s]), "nw": nwrow} for s in sl])
    cnorm = np.concatenate([r["out"] for r in res], 0)

    WoutT = _c(np.asarray(L0_attn_Wout).T)
    bout = _c(L0_attn_bout).reshape(1, -1)
    res = run("lin2048", lambda: build_linproj(D, "lin2048"),
              [{"hT": _c(cnorm[s].T), "WT": WoutT, "bias": bout, "resid": _c(xf[s])}
               for s in sl])
    x1 = np.concatenate([r["out"] for r in res], 0)

    W1T = _c(np.asarray(L0_ffn_W1).T)
    W3T = _c(np.asarray(L0_ffn_W3).T)
    res = run("ffn1", build_ffn1,
              [{"xT": _c(x1[s].T), "W1T": W1T, "W3T": W3T} for s in sl])
    hpre = [r["hout"] for r in res]

    res = run("fwht", build_fwht, [{"hin": _c(h)} for h in hpre])
    hw = [r["hout"] for r in res]

    W2T = _c(np.asarray(L0_ffn_W2).T)
    zb = np.zeros((1, D), np.float32)
    res = run("lin8192", lambda: build_linproj(8192, "lin8192"),
              [{"hT": _c(hw[c].T), "WT": W2T, "bias": zb, "resid": _c(x1[sl[c]])}
               for c in range(NCORE)])
    out = np.concatenate([r["out"] for r in res], 0)
    return out.reshape(2, SEQ, D)
